# version 07
# baseline (speedup 1.0000x reference)
import numpy as np
from contextlib import ExitStack

import concourse.bass as bass
import concourse.mybir as mybir
from concourse.bass_utils import run_bass_kernel_spmd

F32 = mybir.dt.float32
BF16 = mybir.dt.bfloat16
I8 = mybir.dt.int8
AF = mybir.ActivationFunctionType
ALU = mybir.AluOpType
AX = mybir.AxisListType

C = 1024
SCALE = float(np.float32(np.sqrt(1024 * 2048)))
INV_SCALE = float(1.0 / np.float32(np.sqrt(1024 * 2048)))
EPS = 1e-5
MAGIC = 12582912.0

PC_B1Q, PC_B1K, PC_B1V = 0, 8, 16
PC_B2Q, PC_B2K = 24, 32
PC_LNW, PC_LNB = 40, 48
PC_WS, PC_FB = 56, 64
PC_FME, PC_FMO = 72, 80
PC_XS, PC_IXS, PC_XWS = 88, 89, 90
PRM_W = 104


class Sched:

    ENGSEM = {"pe": "spe", "act": "sact", "dve": "sdve", "sy": None, "gp": "sgpc"}

    def __init__(self):
        self.prog = {k: [] for k in ("pe", "act", "dve", "sy", "gp")}
        self.cnt = dict(spe=0, sact=0, sdve=0, sgpc=0, qsy=0, qgp=0, scc=0)
        self.hi = {k: {} for k in self.prog}

    def _wait(self, stream, sem, val):
        if val is None or val <= 0:
            return
        hw = self.hi[stream]
        if hw.get(sem, 0) >= val:
            return
        hw[sem] = val
        self.prog[stream].append(("w", sem, val))

    def stage(self, eng, fn, deps=(), ndma=0, cc=False):
        m = {}
        own = self.ENGSEM[eng]
        flat = []
        for t in deps:
            if isinstance(t, list):
                flat.extend(t)
            else:
                flat.append(t)
        for t in flat:
            if t is None:
                continue
            s, v = t
            if s == own:
                continue
            m[s] = max(m.get(s, 0), v)
        for s, v in sorted(m.items()):
            self._wait(eng, s, v)
        if ndma:
            q = "qsy" if eng == "sy" else "qgp"
            self._wait(eng, q, self.cnt[q])
            self.cnt[q] += 16 * ndma
            self.prog[eng].append(("d", fn, q, ndma))
            return (q, self.cnt[q])
        if cc:
            self.cnt["scc"] += 1
            self.prog[eng].append(("c", fn))
            return ("scc", self.cnt["scc"])
        self.cnt[own] += 1
        self.prog[eng].append(("e", fn, own))
        return (own, self.cnt[own])

    def replay(self, stream, e, H):
        fence = stream in ("dve", "act")
        state = {"prev_compute": False}
        orig = {}

        def wrap(name):
            f = getattr(e, name)

            def g(*a, **kw):
                if state["prev_compute"]:
                    orig["drain"]()
                state["prev_compute"] = True
                return f(*a, **kw)
            return g

        if fence:
            orig["drain"] = e.drain
            for nm in ("tensor_copy", "tensor_tensor", "tensor_scalar",
                       "tensor_add", "tensor_sub", "tensor_mul", "tensor_max",
                       "scalar_tensor_tensor", "tensor_reduce", "reciprocal",
                       "memset", "activation"):
                if hasattr(e, nm):
                    orig[nm] = getattr(e, nm)
                    setattr(e, nm, wrap(nm))
        try:
            for item in self.prog[stream]:
                k = item[0]
                if k == "w":
                    state["prev_compute"] = False
                    e.wait_ge(H[item[1]], item[2])
                elif k == "d":
                    insts = item[1](e)
                    assert len(insts) == item[3], (stream, len(insts), item[3])
                    for inst in insts:
                        inst.then_inc(H[item[2]], 16)
                elif k == "c":
                    item[1](e).then_inc(H["scc"], 1)
                else:
                    inst = item[1](e)
                    assert inst is not None
                    inst.then_inc(H[item[2]], 1)
        finally:
            if fence:
                for nm, f in orig.items():
                    if nm != "drain":
                        setattr(e, nm, f)


def build_program(TCORE, TC, phase_limit=99):
    import os
    SUBA = int(os.environ.get("K_SUBA", "99"))
    SKIP = set(os.environ.get("K_SKIP", "").split(","))
    assert TCORE % TC == 0 and TC % 128 == 0 and TC <= 512
    NCHK = TCORE // TC
    NTT = TC // 128
    nc = bass.Bass("TRN2", target_bir_lowering=False)

    xt = nc.dram_tensor("xt", [C, TCORE], F32, kind="ExternalInput")
    wdr = {}
    for nm in ("w1q", "w2q", "w1k", "w2k", "w1v", "w2v"):
        wdr[nm] = nc.dram_tensor(nm, [C, C], F32, kind="ExternalInput")
    bdr = {}
    for nm in ("bq1", "bq2", "bk1", "bk2", "bv1", "bv2", "lnw", "lnb", "wsc", "fbi"):
        bdr[nm] = nc.dram_tensor(nm, [C], F32, kind="ExternalInput")
    wint = nc.dram_tensor("wint", [C, C], I8, kind="ExternalInput")
    idm = nc.dram_tensor("idm", [128, 128], F32, kind="ExternalInput")
    trm = nc.dram_tensor("trm", [128, 128], F32, kind="ExternalInput")
    fme = nc.dram_tensor("fme", [128, 8], F32, kind="ExternalInput")
    fmo = nc.dram_tensor("fmo", [128, 8], F32, kind="ExternalInput")
    qst = nc.dram_tensor("qst", [C, TCORE], BF16)
    qlst = nc.dram_tensor("qlst", [C, TCORE], BF16)
    wkvb = [nc.dram_tensor("wkvb%d" % j, [C, C], BF16) for j in range(4)]
    ost = nc.dram_tensor("ost", [C, TCORE], BF16)
    xbst = nc.dram_tensor("xbst", [C, TCORE], BF16)
    h2st = nc.dram_tensor("h2st", [C, TCORE], BF16)
    cci = nc.dram_tensor("cci", [2 * C, C], BF16)
    cco = nc.dram_tensor("cco", [2 * C, C], BF16)
    cmi = nc.dram_tensor("cmi", [128], F32)
    cmo = nc.dram_tensor("cmo", [128], F32)
    outt = nc.dram_tensor("outt", [C, TCORE], F32, kind="ExternalOutput")

    es = ExitStack()
    sb = lambda n, s, d: es.enter_context(nc.sbuf_tensor(n, s, d))
    FD = 8 * TC

    W = [sb("w%d" % i, [128, 8192], BF16) for i in range(6)]
    MS = sb("ms", [128, 8192], BF16)
    MSL = sb("msl", [128, 8192], BF16)
    MT32 = sb("mt32", [128, 1024], F32)
    SQ32 = sb("sq32", [128, 2 * TC], F32)
    TA32 = sb("ta32", [128, TC], F32)
    TB32 = sb("tb32", [128, TC], F32)
    TG32 = sb("tg32", [128, 2 * TC], F32)
    XB = sb("xbuf", [128, FD], BF16)
    G = sb("gbuf", [128, FD], BF16)
    GL = sb("glbuf", [128, FD], BF16)
    XO = sb("xobuf", [128, FD], BF16)
    XOL = sb("xolbuf", [128, FD], BF16)
    QF = sb("qfbuf", [128, FD], BF16)
    QL = sb("qlbuf", [128, FD], BF16)
    KF = sb("kfbuf", [128, FD], BF16)
    KT = sb("ktbuf", [128, FD], BF16)
    VT = sb("vtbuf", [128, FD], BF16)
    AS = sb("asbuf", [128, NTT * TC], BF16)
    MB = sb("mbb", [128, TC], F32)
    IB = sb("ibb", [128, TC], F32)
    MIR = sb("mir", [1, 2 * TC], F32)
    ID16 = sb("id16", [128, 128], BF16)
    TR16 = sb("tr16", [128, 128], BF16)
    ONEC = sb("onec", [128, 1], BF16)
    ONEC32 = sb("onec32", [128, 1], F32)
    ONER = sb("oner", [1, 128], BF16)
    ONER32 = sb("oner32", [1, 128], F32)
    XF = sb("xf", [128, FD], F32)
    TIN = sb("tin", [128, max(3 * TC, 1024)], F32)
    PRM = sb("prm", [128, PRM_W], F32)
    B2VB = sb("b2vb", [128, 1024], F32)
    RMX = sb("rmx", [128, NCHK + 1], F32)
    U8 = sb("u8", [128, 2048], I8)
    PB = es.enter_context(nc.psum_tensor("pb", [128, 3584], F32))
    PBT = es.enter_context(nc.psum_tensor("pbt", [128, 1024], BF16))

    H = {n: es.enter_context(nc.semaphore(n))
         for n in ("spe", "sact", "sdve", "sgpc", "qsy", "qgp", "scc")}

    sch = Sched()
    st = sch.stage
    bank_free = {}

    def bank(b):
        return PB[:, b * 512: b * 512 + TC]

    def x2t(f, c):
        return W[f // 2][:, (f % 2) * 4096 + c * TC: (f % 2) * 4096 + (c + 1) * TC]

    WIT, WIN = W[4], W[5]
    R0 = TIN[0:1, 0:TC]
    R1 = TIN[0:1, TC:2 * TC]
    R2 = TIN[0:1, 2 * TC:3 * TC]

    def d_idtr(e):
        return [e.dma_start(out=XF[:, 0:128], in_=idm[:, :]),
                e.dma_start(out=XF[:, 128:256], in_=trm[:, :]),
                e.dma_start(out=PRM[:, PC_FME:PC_FME + 8], in_=fme[:, :]),
                e.dma_start(out=PRM[:, PC_FMO:PC_FMO + 8], in_=fmo[:, :]),
                e.dma_start(out=TIN[0:1, 0:1024], in_=bdr["bv2"].ap().rearrange("(o n) -> o n", o=1))]
    t_idl = st("sy", d_idtr, ndma=5)

    def d_csts(e):
        e.tensor_copy(out=ID16[:, :], in_=XF[:, 0:128])
        e.tensor_copy(out=TR16[:, :], in_=XF[:, 128:256])
        e.memset(ONEC[:, :], 1.0)
        e.memset(ONEC32[:, :], 1.0)
        e.memset(ONER[:, :], 1.0)
        e.memset(ONER32[:, :], 1.0)
        e.memset(MSL[:, :], 0.0)
        return e.memset(MS[:, :], 0.0)
    t_csts = st("dve", d_csts, deps=[t_idl])

    def p_b2v(e):
        e.matmul(out=PB[:, 2048:2560], lhsT=ONER32[0:1, :], rhs=TIN[0:1, 0:512],
                 start=True, stop=True)
        return e.matmul(out=PB[:, 2560:3072], lhsT=ONER32[0:1, :],
                        rhs=TIN[0:1, 512:1024], start=True, stop=True)
    t_b2v = st("pe", p_b2v, deps=[t_idl, t_csts])

    def d_b2ve(e):
        e.tensor_copy(out=B2VB[:, 0:512], in_=PB[:, 2048:2560])
        return e.tensor_copy(out=B2VB[:, 512:1024], in_=PB[:, 2560:3072])
    t_b2ve = st("dve", d_b2ve, deps=[t_b2v])
    bank_free[4] = t_b2ve
    bank_free[5] = t_b2ve

    params = [("bq1", PC_B1Q), ("bk1", PC_B1K), ("bv1", PC_B1V), ("bq2", PC_B2Q),
              ("bk2", PC_B2K), ("lnw", PC_LNW), ("lnb", PC_LNB), ("wsc", PC_WS),
              ("fbi", PC_FB)]
    prm_toks = [t_csts, t_b2ve]
    ping_tok = [None, None]
    t_ev = None
    for i, (nm, col) in enumerate(params):
        off = 256 + (i % 2) * 128

        def d_pl(e, nm=nm, off=off):
            return [e.dma_start(out=XF[0:8, off:off + 128],
                                in_=bdr[nm].ap().rearrange("(h p) -> h p", p=128))]
        t_pl = st("sy", d_pl, deps=[t_idl, ping_tok[i % 2]], ndma=1)

        def p_pt(e, off=off):
            return e.transpose(out=PB[:, 3072:3080], in_=XF[0:8, off:off + 128],
                               identity=XF[0:8, 0:8])
        t_pt = st("pe", p_pt, deps=[t_pl, bank_free.get(6)])

        def d_pe(e, col=col):
            return e.tensor_copy(out=PRM[:, col:col + 8], in_=PB[:, 3072:3080])
        t_ev = st("dve", d_pe, deps=[t_pt])
        bank_free[6] = t_ev
        ping_tok[i % 2] = t_ev
        prm_toks.append(t_ev)

    def d_b2qs(e):
        return e.tensor_scalar(out=PRM[:, PC_B2Q:PC_B2Q + 8],
                               in0=PRM[:, PC_B2Q:PC_B2Q + 8],
                               scalar1=INV_SCALE, scalar2=None, op0=ALU.mult)
    t_b2qs = st("dve", d_b2qs, deps=[t_ev])
    prm_toks.append(t_b2qs)

    wslot_tok = [prm_toks[-1], prm_toks[-1]]
    k = 0
    t_wc = None
    for wi_, nm in ((0, "w1q"), (1, "w2q")):
        for ci in range(8):
            slot = (k % 2) * 1024

            def d_wl(e, nm=nm, slot=slot, ci=ci):
                return [e.dma_start(out=XF[:, slot:slot + 1024],
                                    in_=wdr[nm][ci * 128:(ci + 1) * 128, :])]
            t_wl = st("sy", d_wl, deps=prm_toks + [wslot_tok[k % 2]], ndma=1)

            def d_wc(e, wi_=wi_, ci=ci, slot=slot):
                hi = W[wi_][:, ci * 1024:(ci + 1) * 1024]
                lo = W[wi_ + 2][:, ci * 1024:(ci + 1) * 1024]
                e.tensor_copy(out=hi, in_=XF[:, slot:slot + 1024])
                return e.tensor_sub(out=lo, in0=XF[:, slot:slot + 1024], in1=hi)
            t_wc = st("dve", d_wc, deps=[t_wl])
            wslot_tok[k % 2] = t_wc
            k += 1
    t_kvst = [None] * 4
    for j, nm in enumerate(("w1k", "w2k", "w1v", "w2v")):
        sl = W[4 + (j % 2)]
        cast_toks = []
        for ci in range(8):
            slot = (k % 2) * 1024

            def d_wl(e, nm=nm, slot=slot, ci=ci):
                return [e.dma_start(out=XF[:, slot:slot + 1024],
                                    in_=wdr[nm][ci * 128:(ci + 1) * 128, :])]
            t_wl = st("sy", d_wl, deps=prm_toks + [wslot_tok[k % 2]], ndma=1)

            wc_deps = [t_wl]
            if j >= 2 and ci == 0:
                wc_deps.append(t_kvst[j - 2])

            def d_wc(e, sl=sl, ci=ci, slot=slot):
                return e.tensor_copy(out=sl[:, ci * 1024:(ci + 1) * 1024],
                                     in_=XF[:, slot:slot + 1024])
            t_wc = st("dve", d_wc, deps=wc_deps)
            cast_toks.append(t_wc)
            wslot_tok[k % 2] = t_wc
            k += 1

        def d_wst(e, j=j, sl=sl):
            return [e.dma_start(
                out=wkvb[j].ap().rearrange("(ci p) e -> p ci e", p=128),
                in_=sl.ap().rearrange("p (ci e) -> p ci e", e=1024))]
        t_kvst[j] = st("gp", d_wst, deps=cast_toks, ndma=1)
    t_setup_done = t_wc

    last_tiny_act = [None]
    last_bc = [None]

    last_sq_pe = [None, None]

    def ln_stats(xtiles, deps):
        prev_pe = None
        for f in range(8):
            lo = (f % 2) * TC

            def a_s(e, f=f, lo=lo, xtiles=xtiles):
                return e.activation(out=SQ32[:, lo:lo + TC], in_=xtiles[f],
                                    func=AF.Square)
            t_a = st("act", a_s, deps=list(deps) + [last_sq_pe[f % 2]])

            pdeps = [t_a]
            if prev_pe is None:
                pdeps += list(deps) + [bank_free.get(0), bank_free.get(1)]

            def p_f(e, f=f, lo=lo, xtiles=xtiles):
                e.matmul(out=PB[0:1, 0:TC], lhsT=ONEC[:, 0:1],
                         rhs=xtiles[f], start=(f == 0), stop=(f == 7))
                return e.matmul(out=PB[0:1, 512:512 + TC], lhsT=ONEC32[:, 0:1],
                                rhs=SQ32[:, lo:lo + TC], start=(f == 0),
                                stop=(f == 7))
            prev_pe = st("pe", p_f, deps=pdeps)
            last_sq_pe[f % 2] = prev_pe
        return prev_pe

    def ln_tiny_bcast(t_stats):
        def d_t1(e):
            e.tensor_scalar(out=R0, in0=PB[0:1, 0:TC], scalar1=1.0 / C,
                            scalar2=None, op0=ALU.mult)
            e.tensor_scalar(out=R1, in0=PB[0:1, 512:512 + TC], scalar1=1.0 / C,
                            scalar2=None, op0=ALU.mult)
            e.tensor_copy(out=MIR[0:1, 0:TC], in_=R0)
            e.tensor_tensor(out=R2, in0=R0, in1=R0, op=ALU.mult)
            return e.scalar_tensor_tensor(out=R1, in0=R1, scalar=EPS, in1=R2,
                                          op0=ALU.add, op1=ALU.subtract)
        t1 = st("dve", d_t1, deps=[t_stats, last_tiny_act[0], last_bc[0]])

        def a_t2(e):
            e.activation(out=R0, in_=R1, func=AF.Ln)
            return e.activation(out=MIR[0:1, TC:2 * TC], in_=R0, func=AF.Exp,
                                scale=-0.5)
        t2 = st("act", a_t2, deps=[t1, last_bc[0]])
        last_tiny_act[0] = t2

        def p_bc(e):
            e.matmul(out=PB[:, 0:TC], lhsT=ONER32[0:1, :], rhs=MIR[0:1, 0:TC],
                     start=True, stop=True)
            return e.matmul(out=PB[:, 512:512 + TC], lhsT=ONER32[0:1, :],
                            rhs=MIR[0:1, TC:2 * TC], start=True, stop=True)
        tb = st("pe", p_bc, deps=[t1, t2])
        last_bc[0] = tb

        def d_bce(e):
            e.tensor_copy(out=MB[:, :], in_=PB[:, 0:TC])
            return e.tensor_copy(out=IB[:, :], in_=PB[:, 512:512 + TC])
        te = st("dve", d_bce, deps=[tb])
        bank_free[0] = te
        bank_free[1] = te
        return te

    ln_cast = [None] * NCHK
    ln_app = [None] * NCHK
    ln_stats_t = [None] * NCHK
    ln_bce = [None] * NCHK
    t_xbs_l = [None] * NCHK
    t_qst_l = [None] * NCHK
    t_ost_l = [None] * NCHK
    t_ome_l = [None] * NCHK
    t_qm_l = [None] * NCHK
    t_dm_l = [None] * NCHK
    t_dmu_l = [None] * NCHK
    t_gv_l = [None] * NCHK
    t_ov_l = [None] * NCHK
    t_outq_l = [None] * NCHK
    t_outk_l = [None] * NCHK
    t_evq_l = [None] * NCHK
    t_sq_l = [None] * NCHK

    def ln1_early(c, extra_deps):
        def d_xl(e, c=c):
            return [e.dma_start(
                out=XF.ap().rearrange("p (f t) -> p f t", t=TC),
                in_=xt[:, c * TC:(c + 1) * TC].rearrange("(f p) t -> p f t", p=128))]
        dl = [extra_deps] if isinstance(extra_deps, tuple) else list(extra_deps)
        t_xl = st("sy", d_xl, deps=dl, ndma=1)

        cast_deps = [t_xl]
        if c > 0:
            cast_deps += [t_xbs_l[c - 1], t_sq_l[c - 1], ln_stats_t[c - 1]]

        def d_cast(e):
            return e.tensor_copy(out=XB[:, :], in_=XF[:, :])
        t_cast = st("dve", d_cast, deps=cast_deps)
        ln_cast[c] = t_cast

        def d_xbs(e, c=c):
            return [e.dma_start(
                out=xbst[:, c * TC:(c + 1) * TC].rearrange("(f p) t -> p f t", p=128),
                in_=XB.ap().rearrange("p (f t) -> p f t", t=TC))]
        if "xbs" not in SKIP:
            t_xbs_l[c] = st("gp", d_xbs, deps=[t_cast], ndma=1)
        else:
            t_xbs_l[c] = t_cast

        xtiles = [XB[:, f * TC:(f + 1) * TC] for f in range(8)]
        ln_stats_t[c] = ln_stats(xtiles, [t_cast])
        t_sq_l[c] = ln_stats_t[c]
        ln_bce[c] = ln_tiny_bcast(ln_stats_t[c])

    def ln1_apply(c):
        deps = [ln_bce[c], ln_cast[c]]
        if c > 0:
            deps += [t_ome_l[c - 1], t_ost_l[c - 1]]

        def d_app(e):
            last = None
            for f in range(8):
                e.tensor_sub(out=TA32[:, :], in0=XB[:, f * TC:(f + 1) * TC],
                             in1=MB[:, :])
                e.tensor_mul(out=TB32[:, :], in0=TA32[:, :], in1=IB[:, :])
                e.tensor_copy(out=XO[:, f * TC:(f + 1) * TC], in_=TB32[:, :])
                last = e.tensor_sub(out=XOL[:, f * TC:(f + 1) * TC],
                                    in0=TB32[:, :],
                                    in1=XO[:, f * TC:(f + 1) * TC])
            return last
        ln_app[c] = st("dve", d_app, deps=deps)

    tg_free = [None, None]

    def mlp_q(c, evac_deps_extra):
        t_gel = None
        gel_extra = [t_ov_l[c - 1], t_outq_l[c - 1]] if c > 0 else []
        for h in range(8):
            bg = 2 + (h % 2)

            def p_g(e, h=h, bg=bg):
                last = None
                for term in range(3):
                    w1 = W[0] if term < 2 else W[2]
                    xr = XOL if term == 1 else XO
                    for ci in range(8):
                        last = e.matmul(
                            out=bank(bg),
                            lhsT=w1[:, ci * 1024 + h * 128: ci * 1024 + (h + 1) * 128],
                            rhs=xr[:, ci * TC:(ci + 1) * TC],
                            start=(term == 0 and ci == 0),
                            stop=(term == 2 and ci == 7))
                return last
            t_g = st("pe", p_g, deps=[ln_app[c], bank_free.get(bg)])

            lo = (h % 2) * TC

            def a_gel(e, bg=bg, lo=lo, h=h):
                return e.activation(out=TG32[:, lo:lo + TC], in_=bank(bg),
                                    func=AF.Gelu,
                                    bias=PRM[:, PC_B1Q + h:PC_B1Q + h + 1],
                                    scale=1.0)
            t_ag = st("act", a_gel, deps=[t_g, tg_free[h % 2]])
            bank_free[bg] = t_ag

            def d_gl(e, h=h, lo=lo):
                e.tensor_copy(out=G[:, h * TC:(h + 1) * TC], in_=TG32[:, lo:lo + TC])
                return e.tensor_sub(out=GL[:, h * TC:(h + 1) * TC],
                                    in0=TG32[:, lo:lo + TC],
                                    in1=G[:, h * TC:(h + 1) * TC])
            t_gel = st("dve", d_gl, deps=[t_ag] + gel_extra)
            tg_free[h % 2] = t_gel
        t_out = None
        t_evl = None
        for ech in range(8):
            bg = 2 + (ech % 2)

            def p_q2(e, ech=ech, bg=bg):
                last = None
                for term in range(3):
                    w2 = W[1] if term < 2 else W[3]
                    gr = GL if term == 1 else G
                    for h in range(8):
                        last = e.matmul(
                            out=bank(bg),
                            lhsT=w2[:, h * 1024 + ech * 128: h * 1024 + (ech + 1) * 128],
                            rhs=gr[:, h * TC:(h + 1) * TC],
                            start=(term == 0 and h == 0),
                            stop=(term == 2 and h == 7))
                return last
            t_out = st("pe", p_q2, deps=[t_gel, bank_free.get(bg)])

            lo = (ech % 2) * TC

            def a_q(e, ech=ech, bg=bg, lo=lo):
                return e.activation(out=TG32[:, lo:lo + TC], in_=bank(bg),
                                    func=AF.Identity,
                                    bias=PRM[:, PC_B2Q + ech:PC_B2Q + ech + 1],
                                    scale=INV_SCALE)
            t_aq = st("act", a_q, deps=[t_out, tg_free[ech % 2]])
            bank_free[bg] = t_aq

            def d_q(e, ech=ech, lo=lo):
                e.tensor_copy(out=QF[:, ech * TC:(ech + 1) * TC],
                              in_=TG32[:, lo:lo + TC])
                return e.tensor_sub(out=QL[:, ech * TC:(ech + 1) * TC],
                                    in0=TG32[:, lo:lo + TC],
                                    in1=QF[:, ech * TC:(ech + 1) * TC])
            t_evl = st("dve", d_q, deps=[t_aq] + evac_deps_extra)
            tg_free[ech % 2] = t_evl
        return t_gel, t_out, t_evl

    def mlp_block(c, w1, w2, b1col, outbuf, evac_kind, evac_deps_extra,
                  w1tok=None, w2tok=None):
        import os
        SUBG = int(os.environ.get("K_SUBG", "8")) if evac_kind == "v" else 8
        t_gel = None
        gel_deps0 = [t_ov_l[c - 1]] if c > 0 else []
        import os as _os2
        _bko = _os2.environ.get("K_BANK", "") if evac_kind == "v" else ""
        _noev = (_os2.environ.get("K_GELKIND", "") == "none") if evac_kind == "v" else False
        for h in range(SUBG):
            bg = 2 + (h % 2)
            if _bko:
                bg = int(_bko) + (h % 2)

            def p_g(e, h=h, bg=bg, w1=w1):
                last = None
                for ci in range(8):
                    last = e.matmul(out=bank(bg),
                                    lhsT=w1[:, ci * 1024 + h * 128: ci * 1024 + (h + 1) * 128],
                                    rhs=XO[:, ci * TC:(ci + 1) * TC],
                                    start=(ci == 0), stop=(ci == 7))
                return last
            t_g = st("pe", p_g, deps=[ln_app[c], bank_free.get(bg), w1tok])
            if _noev:
                t_gel = t_g
                bank_free[bg] = t_g
                continue

            import os as _os
            GK = _os.environ.get("K_GELKIND", "gelu") if evac_kind == "v" else "gelu"
            if GK == "dve":
                def a_gel(e, h=h, bg=bg):
                    return e.tensor_copy(out=G[:, h * TC:(h + 1) * TC], in_=bank(bg))
                t_gel = st("dve", a_gel, deps=[t_g] + gel_deps0)
            elif GK == "ident":
                def a_gel(e, h=h, bg=bg, b1col=b1col):
                    return e.activation(out=G[:, h * TC:(h + 1) * TC], in_=bank(bg),
                                        func=AF.Identity,
                                        bias=PRM[:, b1col + h:b1col + h + 1], scale=1.0)
                t_gel = st("act", a_gel, deps=[t_g] + gel_deps0)
            else:
                def a_gel(e, h=h, bg=bg, b1col=b1col):
                    return e.activation(out=G[:, h * TC:(h + 1) * TC], in_=bank(bg),
                                        func=AF.Gelu,
                                        bias=PRM[:, b1col + h:b1col + h + 1], scale=1.0)
                t_gel = st("act", a_gel, deps=[t_g] + gel_deps0)
            bank_free[bg] = t_gel
        t_out = None
        t_evl = None
        if evac_kind == "v":
            import os
            SUBV = int(os.environ.get("K_SUBV", "99"))
            if SUBV < 1:
                return t_gel, t_gel, t_gel
            for gi in range(2 * NTT):
                tt, eh = gi // 2, gi % 2
                bg = 2 + (gi % 2)

                def p_v(e, tt=tt, eh=eh, bg=bg, w2=w2):
                    last = None
                    for h in range(8):
                        last = e.matmul(
                            out=PB[:, bg * 512:bg * 512 + 512],
                            lhsT=G[:, h * TC + tt * 128: h * TC + (tt + 1) * 128],
                            rhs=w2[:, h * 1024 + eh * 512: h * 1024 + (eh + 1) * 512],
                            start=(h == 0), stop=(h == 7))
                    return last
                t_out = st("pe", p_v, deps=[t_gel, bank_free.get(bg), w2tok])

                if SUBV < 2:
                    def d_ve(e, tt=tt, eh=eh, bg=bg):
                        return e.tensor_copy(
                            out=VT[:, tt * 1024 + eh * 512: tt * 1024 + (eh + 1) * 512],
                            in_=PB[:, bg * 512:bg * 512 + 512])
                else:
                    def d_ve(e, tt=tt, eh=eh, bg=bg):
                        return e.scalar_tensor_tensor(
                            out=VT[:, tt * 1024 + eh * 512: tt * 1024 + (eh + 1) * 512],
                            in0=PB[:, bg * 512:bg * 512 + 512], scalar=0.0,
                            in1=B2VB[:, eh * 512:(eh + 1) * 512],
                            op0=ALU.bypass, op1=ALU.add)
                t_evl = st("dve", d_ve, deps=[t_out] + evac_deps_extra)
                bank_free[bg] = t_evl
        else:
            for ech in range(8):
                bg = 2 + (ech % 2)

                def p_o(e, ech=ech, bg=bg, w2=w2):
                    last = None
                    for h in range(8):
                        last = e.matmul(
                            out=bank(bg),
                            lhsT=w2[:, h * 1024 + ech * 128: h * 1024 + (ech + 1) * 128],
                            rhs=G[:, h * TC:(h + 1) * TC],
                            start=(h == 0), stop=(h == 7))
                    return last
                t_out = st("pe", p_o, deps=[t_gel, bank_free.get(bg), w2tok])

                if evac_kind == "q":
                    def a_e(e, ech=ech, bg=bg):
                        return e.activation(out=QF[:, ech * TC:(ech + 1) * TC],
                                            in_=bank(bg), func=AF.Identity,
                                            bias=PRM[:, PC_B2Q + ech:PC_B2Q + ech + 1],
                                            scale=INV_SCALE)
                else:
                    def a_e(e, ech=ech, bg=bg):
                        return e.activation(out=KF[:, ech * TC:(ech + 1) * TC],
                                            in_=bank(bg), func=AF.Identity,
                                            bias=PRM[:, PC_B2K + ech:PC_B2K + ech + 1],
                                            scale=1.0)
                t_evl = st("act", a_e, deps=[t_out] + evac_deps_extra)
                bank_free[bg] = t_evl
        return t_gel, t_out, t_evl

    if phase_limit < 1:
        NCHK_RUN = 0
    else:
        NCHK_RUN = NCHK
    if NCHK_RUN:
        ln1_early(0, [t_setup_done])
        ln1_apply(0)

    def d_wsl(e, j):
        sl = W[4 + (j % 2)]
        return [e.dma_start(
            out=sl.ap().rearrange("p (ci e) -> p ci e", e=1024),
            in_=wkvb[j].ap().rearrange("(ci p) e -> p ci e", p=128))]

    for c in range(NCHK_RUN):
        ws0_deps = [t_kvst[0]] + ([t_gv_l[c - 1]] if c > 0 else [])
        t_ws0 = st("sy", lambda e: d_wsl(e, 0), deps=ws0_deps, ndma=1)
        ws1_deps = [t_kvst[1]] + ([t_ov_l[c - 1]] if c > 0 else [])
        t_ws1 = st("sy", lambda e: d_wsl(e, 1), deps=ws1_deps, ndma=1)

        evq_extra = []
        if c > 0:
            evq_extra = [t_qm_l[c - 1], t_qst_l[c - 1]]
        if SUBA < 1:
            continue
        t_gq, t_outq, t_evq = mlp_q(c, evq_extra)
        t_evq_l[c] = t_evq
        t_outq_l[c] = t_outq
        if SUBA < 2:
            continue

        if c + 1 < NCHK:
            ln1_early(c + 1, [ln_cast[c]])

        def d_qst(e, c=c):
            return [e.dma_start(
                out=qst[:, c * TC:(c + 1) * TC].rearrange("(f p) t -> p f t", p=128),
                in_=QF.ap().rearrange("p (f t) -> p f t", t=TC)),
                e.dma_start(
                out=qlst[:, c * TC:(c + 1) * TC].rearrange("(f p) t -> p f t", p=128),
                in_=QL.ap().rearrange("p (f t) -> p f t", t=TC))]
        if "qst" not in SKIP:
            t_qst_l[c] = st("gp", d_qst, deps=[t_evq], ndma=2)
        else:
            t_qst_l[c] = t_evq
        if SUBA < 3:
            continue

        evk_extra = [t_qm_l[c - 1]] if c > 0 else []
        t_gk, t_outk, t_evk = mlp_block(c, W[4], W[5], PC_B1K, KF, "k", evk_extra,
                                        w1tok=t_ws0, w2tok=t_ws1)
        t_outk_l[c] = t_outk

        t_ws2 = st("sy", lambda e: d_wsl(e, 2), deps=[t_kvst[2], t_gk], ndma=1)
        t_ws3 = st("sy", lambda e: d_wsl(e, 3), deps=[t_kvst[3], t_outk], ndma=1)
        if SUBA < 4:
            continue

        t_tre = None
        for d in (() if "kt" in SKIP else range(8)):
            def p_tr(e, d=d, c=c):
                last = None
                for tt in range(NTT):
                    last = e.transpose(out=PBT[:, tt * 128:(tt + 1) * 128],
                                       in_=KF[:, d * TC + tt * 128: d * TC + (tt + 1) * 128],
                                       identity=ID16[:, :])
                return last
            tr_deps = [t_evk, bank_free.get("t")]
            t_tr = st("pe", p_tr, deps=tr_deps)

            tre_deps = [t_tr]
            if c > 0:
                tre_deps.append(t_dm_l[c - 1])

            def d_tre(e, d=d):
                last = None
                for tt in range(NTT):
                    last = e.tensor_copy(
                        out=KT[:, tt * 1024 + d * 128: tt * 1024 + (d + 1) * 128],
                        in_=PBT[:, tt * 128:(tt + 1) * 128])
                return last
            t_tre = st("dve", d_tre, deps=tre_deps)
            bank_free["t"] = t_tre
        t_kt_done = t_tre
        if SUBA < 5:
            continue

        evv_extra = [t_dm_l[c - 1]] if c > 0 else []
        t_gv, t_ov, t_evv = mlp_block(c, W[4], W[5], PC_B1V, VT, "v", evv_extra,
                                      w1tok=t_ws2, w2tok=t_ws3)
        t_gv_l[c] = t_gv
        t_ov_l[c] = t_ov
        if SUBA < 6:
            continue

        SUBT = int(os.environ.get("K_SUBT", "99"))
        ATEV = os.environ.get("K_ATEV", "full")
        t_am_toks = []
        bank6_free = [bank_free.get(6)] if not isinstance(bank_free.get(6), list) \
            else bank_free.get(6)
        for jt in range(min(NTT, SUBT)):
            w_ = TC - jt * 128

            def p_a(e, jt=jt, w_=w_):
                last = None
                for qr in (QF, QL):
                    for d in range(8):
                        last = e.matmul(
                            out=PB[:, 3072:3072 + w_],
                            lhsT=KF[:, d * TC + jt * 128: d * TC + (jt + 1) * 128],
                            rhs=qr[:, d * TC + jt * 128:(d + 1) * TC],
                            start=(qr is QF and d == 0),
                            stop=(qr is QL and d == 7))
                return last
            a_deps = [t_evq, t_evk] + list(bank6_free)
            if c > 0:
                a_deps.append(t_qm_l[c - 1])
            t_a = st("pe", p_a, deps=a_deps)

            bank6_free = []

            if ATEV == "copyall":
                def d_am(e, jt=jt, w_=w_):
                    return e.tensor_copy(out=AS[:, jt * TC: jt * TC + w_],
                                         in_=PB[:, 3072:3072 + w_])
                t_am = st("dve", d_am, deps=[t_a])
                t_am_toks.append(t_am)
                bank6_free.append(t_am)
                bank_free[6] = list(bank6_free)
                continue
            def d_am(e, jt=jt):
                return e.tensor_mul(out=AS[:, jt * TC: jt * TC + 128],
                                    in0=PB[:, 3072:3072 + 128], in1=TR16[:, :])
            t_am = st("dve", d_am, deps=[t_a])
            t_am_toks.append(t_am)
            bank6_free.append(t_am)
            if w_ > 128 and ATEV != "nocopy":
                def a_ac(e, jt=jt, w_=w_):
                    return e.tensor_copy(out=AS[:, jt * TC + 128: jt * TC + w_],
                                         in_=PB[:, 3072 + 128:3072 + w_])
                t_ac = st("dve", a_ac, deps=[t_a])
                t_am_toks.append(t_ac)
                bank6_free.append(t_ac)
            bank_free[6] = list(bank6_free)

        if SUBA < 7:
            continue
        t_ome = None
        t_qm = None
        for ech in range(8):
            bo = 2 + (ech % 2)
            qm_deps = [t_evv] + t_am_toks + [bank_free.get(bo)]
            if c > 0:
                qm_deps.append(t_dmu_l[c - 1])

            def p_qm(e, ech=ech, bo=bo, c=c):
                last = None
                if c > 0:
                    for ms_, qr in ((MS, QF), (MSL, QF), (MS, QL)):
                        for d in range(8):
                            last = e.matmul(
                                out=bank(bo),
                                lhsT=ms_[:, d * 1024 + ech * 128: d * 1024 + (ech + 1) * 128],
                                rhs=qr[:, d * TC:(d + 1) * TC],
                                start=(ms_ is MS and qr is QF and d == 0),
                                stop=False)
                for jt in range(NTT):
                    w_ = TC - jt * 128
                    last = e.matmul(
                        out=PB[:, bo * 512 + jt * 128: bo * 512 + TC],
                        lhsT=VT[:, jt * 1024 + ech * 128: jt * 1024 + (ech + 1) * 128],
                        rhs=AS[:, jt * TC: jt * TC + w_],
                        start=(c == 0 and jt == 0), stop=(jt == NTT - 1))
                return last
            t_qm = st("pe", p_qm, deps=qm_deps)

            ome_deps = [t_qm, t_gv]
            if c > 0:
                ome_deps.append(t_ost_l[c - 1])

            def a_ome(e, ech=ech, bo=bo):
                return e.activation(out=XO[:, ech * TC:(ech + 1) * TC], in_=bank(bo),
                                    func=AF.Identity)
            t_ome = st("act", a_ome, deps=ome_deps)
            bank_free[bo] = t_ome
        t_qm_l[c] = t_qm
        t_ome_l[c] = t_ome

        def d_ost(e, c=c):
            return [e.dma_start(
                out=ost[:, c * TC:(c + 1) * TC].rearrange("(f p) t -> p f t", p=128),
                in_=XO.ap().rearrange("p (f t) -> p f t", t=TC))]
        t_ost_l[c] = st("gp", d_ost, deps=[t_ome], ndma=1)

        if SUBA < 8:
            continue
        t_dm = None
        t_dmu = None
        for d in range(8):
            pair = 4 if d % 2 == 0 else 2
            base = pair * 512
            dm_deps = [t_kt_done, t_evv, bank_free.get(pair), bank_free.get(pair + 1)]

            def p_dm(e, d=d, base=base):
                last = None
                for eh in range(2):
                    for tt in range(NTT):
                        last = e.matmul(
                            out=PB[:, base + eh * 512: base + (eh + 1) * 512],
                            lhsT=KT[:, tt * 1024 + d * 128: tt * 1024 + (d + 1) * 128],
                            rhs=VT[:, tt * 1024 + eh * 512: tt * 1024 + (eh + 1) * 512],
                            start=(tt == 0), stop=(tt == NTT - 1))
                return last
            t_dm = st("pe", p_dm, deps=dm_deps)

            def d_dmu(e, d=d, base=base):
                e.tensor_add(out=MT32[:, :], in0=PB[:, base: base + 1024],
                             in1=MSL[:, d * 1024:(d + 1) * 1024])
                e.tensor_add(out=MT32[:, :], in0=MT32[:, :],
                             in1=MS[:, d * 1024:(d + 1) * 1024])
                e.tensor_copy(out=MS[:, d * 1024:(d + 1) * 1024], in_=MT32[:, :])
                return e.tensor_sub(out=MSL[:, d * 1024:(d + 1) * 1024],
                                    in0=MT32[:, :],
                                    in1=MS[:, d * 1024:(d + 1) * 1024])
            t_dmu = st("dve", d_dmu, deps=[t_dm, t_qm_l[c]])
            bank_free[pair] = t_dmu
            bank_free[pair + 1] = t_dmu
        t_dm_l[c] = t_dm
        t_dmu_l[c] = t_dmu

        if c + 1 < NCHK:
            ln1_apply(c + 1)

    if phase_limit < 2:
        sch._wait("sy", "qgp", sch.cnt["qgp"])
        sch._wait("sy", "qsy", sch.cnt["qsy"])
        return _finish(nc, sch, H, es)
    def d_mask(e):
        e.tensor_scalar(out=MS[:, :], in0=MS[:, :],
                        scalar1=PRM[:, PC_FME:PC_FME + 1], scalar2=None,
                        op0=ALU.mult)
        return e.tensor_scalar(out=MSL[:, :], in0=MSL[:, :],
                               scalar1=PRM[:, PC_FME:PC_FME + 1], scalar2=None,
                               op0=ALU.mult)
    t_mask = st("dve", d_mask, deps=[t_dmu_l[NCHK - 1], t_qm_l[NCHK - 1]])

    def d_cci(e):
        return [e.dma_start(out=cci[0:1024, :].rearrange("(d p) e -> p d e", p=128),
                            in_=MS.ap().rearrange("p (d e) -> p d e", e=1024)),
                e.dma_start(out=cci[1024:2048, :].rearrange("(d p) e -> p d e", p=128),
                            in_=MSL.ap().rearrange("p (d e) -> p d e", e=1024))]
    t_cci = st("gp", d_cci, deps=[t_mask], ndma=2)

    def g_cc1(e):
        return e.collective_compute(
            "AllReduce", ALU.add,
            replica_groups=[[0, 1], [2, 3], [4, 5], [6, 7]],
            ins=[cci.ap().opt()], outs=[cco.ap().opt()])
    t_cc1 = st("gp", g_cc1, deps=[t_cci], cc=True)

    def d_cco(e):
        return [e.dma_start(out=MS.ap().rearrange("p (d e) -> p d e", e=1024),
                            in_=cco[0:1024, :].rearrange("(d p) e -> p d e", p=128)),
                e.dma_start(out=MSL.ap().rearrange("p (d e) -> p d e", e=1024),
                            in_=cco[1024:2048, :].rearrange("(d p) e -> p d e", p=128))]
    t_cco = st("sy", d_cco, deps=[t_cc1], ndma=2)

    def d_smask(e):
        e.tensor_scalar(out=MS[:, :], in0=MS[:, :],
                        scalar1=PRM[:, PC_FMO:PC_FMO + 1], scalar2=None,
                        op0=ALU.mult)
        return e.tensor_scalar(out=MSL[:, :], in0=MSL[:, :],
                               scalar1=PRM[:, PC_FMO:PC_FMO + 1], scalar2=None,
                               op0=ALU.mult)
    t_smask = st("dve", d_smask, deps=[t_cco])

    if phase_limit < 3:
        sch._wait("sy", "qgp", sch.cnt["qgp"])
        sch._wait("sy", "qsy", sch.cnt["qsy"])
        return _finish(nc, sch, H, es)
    t_phaseA_pe = t_dm_l[NCHK - 1]
    wl_tok = [t_ov_l[NCHK - 1], t_ov_l[NCHK - 1]]
    t_wic_toks = []
    for i in range(8):
        slot = (i % 2) * 1024

        def d_wil(e, i=i, slot=slot):
            return [e.dma_start(out=U8[:, slot:slot + 1024],
                                in_=wint[i * 128:(i + 1) * 128, :])]
        t_wil = st("sy", d_wil, deps=[wl_tok[i % 2]], ndma=1)

        def d_wic(e, i=i, slot=slot):
            return e.tensor_copy(out=WIN[:, i * 1024:(i + 1) * 1024],
                                 in_=U8[:, slot:slot + 1024])
        t_wic = st("dve", d_wic, deps=[t_wil, t_ov_l[NCHK - 1]])
        wl_tok[i % 2] = t_wic
        t_wic_toks.append(t_wic)

    t_wit_evacs = []
    for i in range(8):
        for jq in range(2):
            def p_wt(e, i=i, jq=jq):
                last = None
                for k2 in range(4):
                    j = jq * 4 + k2
                    last = e.transpose(
                        out=PBT[:, (j % 4) * 128:((j % 4) + 1) * 128],
                        in_=WIN[:, i * 1024 + j * 128: i * 1024 + (j + 1) * 128],
                        identity=ID16[:, :])
                return last
            t_wt = st("pe", p_wt, deps=[t_wic_toks[i], bank_free.get("t"),
                                        t_gv_l[NCHK - 1]])

            def d_wte(e, i=i, jq=jq):
                last = None
                for k2 in range(4):
                    j = jq * 4 + k2
                    last = e.tensor_copy(
                        out=WIT[:, j * 1024 + i * 128: j * 1024 + (i + 1) * 128],
                        in_=PBT[:, (j % 4) * 128:((j % 4) + 1) * 128])
                return last
            t_wte = st("dve", d_wte, deps=[t_wt])
            bank_free["t"] = t_wte
            t_wit_evacs.append(t_wte)
    t_wit_done = t_wit_evacs[-1]

    if phase_limit < 4:
        sch._wait("sy", "qgp", sch.cnt["qgp"])
        sch._wait("sy", "qsy", sch.cnt["qsy"])
        return _finish(nc, sch, H, es)
    qgp_after_A = ("qgp", sch.cnt["qgp"])
    t_corr_prev = None
    t_stats_a_prev = None
    t_sqa_prev = None
    t_stats_b_prev = None
    t_stats_c_prev = None
    t_h2s_prev = None
    t_x2_toks = [None] * NCHK
    t_h2s_toks = []
    t_mx_toks = []

    for c in range(NCHK):
        ql_deps = [qgp_after_A, t_corr_prev or t_qm_l[NCHK - 1]]

        def d_ql(e, c=c):
            return [e.dma_start(
                out=QF.ap().rearrange("p (f t) -> p f t", t=TC),
                in_=qst[:, c * TC:(c + 1) * TC].rearrange("(f p) t -> p f t", p=128)),
                e.dma_start(
                out=QL.ap().rearrange("p (f t) -> p f t", t=TC),
                in_=qlst[:, c * TC:(c + 1) * TC].rearrange("(f p) t -> p f t", p=128))]
        t_ql = st("sy", d_ql, deps=ql_deps + [t_evq_l[NCHK - 1]], ndma=2)

        def d_ol(e, c=c):
            return [e.dma_start(
                out=KF.ap().rearrange("p (f t) -> p f t", t=TC),
                in_=ost[:, c * TC:(c + 1) * TC].rearrange("(f p) t -> p f t", p=128))]
        ol_deps = [qgp_after_A, t_qm_l[NCHK - 1]]
        if c > 0:
            ol_deps.append(t_stats_a_prev)
        t_ol = st("sy", d_ol, deps=ol_deps, ndma=1)

        def d_xbl(e, c=c):
            return [e.dma_start(
                out=XB.ap().rearrange("p (f t) -> p f t", t=TC),
                in_=xbst[:, c * TC:(c + 1) * TC].rearrange("(f p) t -> p f t", p=128))]
        xbl_deps = [qgp_after_A, ln_stats_t[NCHK - 1], t_sq_l[NCHK - 1]]
        if c > 0:
            xbl_deps.append(t_x2_toks[c - 1])
        t_xbl = st("sy", d_xbl, deps=xbl_deps, ndma=1)

        t_of_toks = []
        for ech in range(8):
            bo = 2 + (ech % 2)

            def p_cor(e, ech=ech, bo=bo):
                last = None
                for ms_, qr in ((MS, QF), (MSL, QF), (MS, QL)):
                    for d in range(8):
                        last = e.matmul(
                            out=bank(bo),
                            lhsT=ms_[:, d * 1024 + ech * 128: d * 1024 + (ech + 1) * 128],
                            rhs=qr[:, d * TC:(d + 1) * TC],
                            start=(ms_ is MS and qr is QF and d == 0),
                            stop=(ms_ is MS and qr is QL and d == 7))
                return last
            t_cor = st("pe", p_cor, deps=[t_smask, t_ql, bank_free.get(bo)])

            of_deps = [t_cor, t_ol]
            if c > 0:
                of_deps += [t_stats_a_prev]

            def d_of(e, ech=ech, bo=bo):
                return e.tensor_add(out=G[:, ech * TC:(ech + 1) * TC],
                                    in0=bank(bo), in1=KF[:, ech * TC:(ech + 1) * TC])
            t_of = st("dve", d_of, deps=of_deps)
            bank_free[bo] = t_of
            t_of_toks.append(t_of)
        t_corr_prev = t_cor

        gt = [G[:, f * TC:(f + 1) * TC] for f in range(8)]
        t_st_a = ln_stats(gt, [t_of_toks[-1]])
        t_stats_a_prev = t_st_a
        t_bce_a = ln_tiny_bcast(t_st_a)

        def d_z1(e):
            last = None
            for f in range(8):
                e.tensor_sub(out=TA32[:, :], in0=G[:, f * TC:(f + 1) * TC],
                             in1=MB[:, :])
                last = e.tensor_mul(out=G[:, f * TC:(f + 1) * TC],
                                    in0=TA32[:, :], in1=IB[:, :])
            return last
        t_z1 = st("dve", d_z1, deps=[t_bce_a, t_st_a])

        z2_deps = [t_z1]
        if c > 0:
            z2_deps.append(t_stats_b_prev)

        def d_z2(e):
            last = None
            for f in range(8):
                last = e.tensor_scalar(out=KT[:, f * TC:(f + 1) * TC],
                                       in0=G[:, f * TC:(f + 1) * TC],
                                       scalar1=PRM[:, PC_LNW + f:PC_LNW + f + 1],
                                       scalar2=PRM[:, PC_LNB + f:PC_LNB + f + 1],
                                       op0=ALU.mult, op1=ALU.add)
            return last
        t_z2 = st("dve", d_z2, deps=z2_deps)

        ktt = [KT[:, f * TC:(f + 1) * TC] for f in range(8)]
        t_st_b = ln_stats(ktt, [t_z2])
        t_stats_b_prev = t_st_b
        t_bce_b = ln_tiny_bcast(t_st_b)

        def d_z3(e):
            last = None
            for f in range(8):
                e.tensor_sub(out=TA32[:, :], in0=KT[:, f * TC:(f + 1) * TC],
                             in1=MB[:, :])
                last = e.tensor_mul(out=KT[:, f * TC:(f + 1) * TC],
                                    in0=TA32[:, :], in1=IB[:, :])
            return last
        t_z3 = st("dve", d_z3, deps=[t_bce_b, t_st_b])

        x2_deps = [t_z3, t_xbl]
        if c == 0:
            x2_deps += [t_outq_l[NCHK - 1], t_qm_l[NCHK - 1]]
        else:
            x2_deps.append(t_stats_c_prev)

        def d_x2(e, c=c):
            last = None
            for f in range(8):
                last = e.tensor_add(out=x2t(f, c), in0=XB[:, f * TC:(f + 1) * TC],
                                    in1=KT[:, f * TC:(f + 1) * TC])
            return last
        t_x2 = st("dve", d_x2, deps=x2_deps)
        t_x2_toks[c] = t_x2

        x2tiles = [x2t(f, c) for f in range(8)]
        t_st_c = ln_stats(x2tiles, [t_x2])
        t_stats_c_prev = t_st_c
        t_bce_c = ln_tiny_bcast(t_st_c)

        h2_deps = [t_bce_c, t_st_c]
        if c == 0:
            h2_deps.append(t_dm_l[NCHK - 1])
        else:
            h2_deps.append(t_h2s_prev)

        def d_h2(e, c=c):
            last = None
            for f in range(8):
                e.tensor_sub(out=TA32[:, :], in0=x2t(f, c), in1=MB[:, :])
                last = e.tensor_mul(out=VT[:, f * TC:(f + 1) * TC],
                                    in0=TA32[:, :], in1=IB[:, :])
            return last
        t_h2 = st("dve", d_h2, deps=h2_deps)

        def d_mx(e, c=c):
            return e.tensor_reduce(out=RMX[:, c:c + 1], in_=VT[:, :], axis=AX.X,
                                   op=ALU.max, apply_absolute_value=True)
        t_mx = st("dve", d_mx, deps=[t_h2])
        t_mx_toks.append(t_mx)

        def d_h2s(e, c=c):
            return [e.dma_start(
                out=h2st[:, c * TC:(c + 1) * TC].rearrange("(f p) t -> p f t", p=128),
                in_=VT.ap().rearrange("p (f t) -> p f t", t=TC))]
        t_h2s = st("gp", d_h2s, deps=[t_h2], ndma=1)
        t_h2s_prev = t_h2s
        t_h2s_toks.append(t_h2s)

    if phase_limit < 5:
        sch._wait("sy", "qgp", sch.cnt["qgp"])
        sch._wait("sy", "qsy", sch.cnt["qsy"])
        return _finish(nc, sch, H, es)
    def d_fmx(e):
        return e.tensor_reduce(out=RMX[:, NCHK:NCHK + 1], in_=RMX[:, 0:NCHK],
                               axis=AX.X, op=ALU.max)
    t_fmx = st("dve", d_fmx, deps=t_mx_toks)

    def d_cmi(e):
        return [e.dma_start(out=cmi.ap().rearrange("(p o) -> p o", o=1),
                            in_=RMX[:, NCHK:NCHK + 1])]
    t_cmi = st("gp", d_cmi, deps=[t_fmx], ndma=1)

    def g_cc2(e):
        return e.collective_compute(
            "AllReduce", ALU.max, replica_groups=[[0, 1, 2, 3, 4, 5, 6, 7]],
            ins=[cmi.ap().opt()], outs=[cmo.ap().opt()])
    t_cc2 = st("gp", g_cc2, deps=[t_cmi], cc=True)

    def d_cmo(e):
        return [e.dma_start(out=TIN[0:1, 0:128],
                            in_=cmo.ap().rearrange("(o p) -> o p", o=1))]
    t_cmo = st("sy", d_cmo, deps=[t_cc2, last_tiny_act[0], ("sdve", sch.cnt["sdve"])],
               ndma=1)

    def d_xs(e):
        e.tensor_reduce(out=TIN[0:1, 200:201], in_=TIN[0:1, 0:128], axis=AX.X,
                        op=ALU.max)
        e.tensor_scalar(out=TIN[0:1, 201:202], in0=TIN[0:1, 200:201],
                        scalar1=float(np.float64(1.0) / np.float64(127.0)),
                        scalar2=1e-5, op0=ALU.mult, op1=ALU.max)
        return e.reciprocal(out=TIN[0:1, 202:203], in_=TIN[0:1, 201:202])
    t_xs = st("dve", d_xs, deps=[t_cmo])

    def p_bcx(e):
        return e.matmul(out=PB[:, 0:2], lhsT=ONER32[0:1, :], rhs=TIN[0:1, 201:203],
                        start=True, stop=True)
    t_bcx = st("pe", p_bcx, deps=[t_xs, bank_free.get(0)])

    def d_bcxe(e):
        e.tensor_copy(out=PRM[:, PC_XS:PC_XS + 2], in_=PB[:, 0:2])
        return e.tensor_scalar(out=PRM[:, PC_XWS:PC_XWS + 8],
                               in0=PRM[:, PC_WS:PC_WS + 8],
                               scalar1=PRM[:, PC_XS:PC_XS + 1], scalar2=None,
                               op0=ALU.mult)
    t_bcxe = st("dve", d_bcxe, deps=[t_bcx])
    bank_free[0] = t_bcxe

    if phase_limit < 6:
        sch._wait("sy", "qgp", sch.cnt["qgp"])
        sch._wait("sy", "qsy", sch.cnt["qsy"])
        return _finish(nc, sch, H, es)
    qgp_after_A2 = ("qgp", sch.cnt["qgp"])
    t_ffn_prev = None
    t_q2_prev = None
    t_stats_d_prev = None
    t_os_ping = [None, None]
    t_dq_prev = None

    for c in range(NCHK):
        hl_deps = [qgp_after_A2, t_mx_toks[-1]]
        if c > 0:
            hl_deps += [t_ffn_prev, t_q2_prev]

        def d_hl(e, c=c):
            return [e.dma_start(
                out=VT.ap().rearrange("p (f t) -> p f t", t=TC),
                in_=h2st[:, c * TC:(c + 1) * TC].rearrange("(f p) t -> p f t", p=128))]
        t_hl = st("sy", d_hl, deps=hl_deps, ndma=1)

        q1_deps = [t_hl, t_bcxe, t_os_ping[0], t_os_ping[1]]

        def d_q1(e):
            return e.tensor_scalar(out=XF[:, :], in0=VT[:, :],
                                   scalar1=PRM[:, PC_IXS:PC_IXS + 1], scalar2=None,
                                   op0=ALU.mult)
        t_q1 = st("dve", d_q1, deps=q1_deps)

        def d_q2(e):
            return e.tensor_scalar(out=VT[:, :], in0=XF[:, :], scalar1=MAGIC,
                                   scalar2=MAGIC, op0=ALU.add, op1=ALU.subtract)
        t_q2 = st("dve", d_q2, deps=[t_q1])
        t_q2_prev = t_q2

        t_dq_toks = []
        for n in range(8):
            bo = 2 + (n % 2)

            def p_ffn(e, n=n, bo=bo):
                last = None
                for ci in range(8):
                    last = e.matmul(
                        out=bank(bo),
                        lhsT=WIT[:, ci * 1024 + n * 128: ci * 1024 + (n + 1) * 128],
                        rhs=VT[:, ci * TC:(ci + 1) * TC],
                        start=(ci == 0), stop=(ci == 7))
                return last
            t_ffn = st("pe", p_ffn, deps=[t_q2, t_wit_done, bank_free.get(bo)])

            dq_deps = [t_ffn, t_bcxe]
            if c > 0:
                dq_deps.append(t_stats_d_prev)
            elif c == 0:
                dq_deps.append(t_corr_prev)

            def d_dq(e, n=n, bo=bo):
                return e.tensor_scalar(out=QF[:, n * TC:(n + 1) * TC], in0=bank(bo),
                                       scalar1=PRM[:, PC_XWS + n:PC_XWS + n + 1],
                                       scalar2=PRM[:, PC_FB + n:PC_FB + n + 1],
                                       op0=ALU.mult, op1=ALU.add)
            t_dq = st("dve", d_dq, deps=dq_deps)
            bank_free[bo] = t_dq
            t_dq_toks.append(t_dq)
        t_ffn_prev = t_ffn
        t_dq_prev = t_dq_toks[-1]

        qft = [QF[:, f * TC:(f + 1) * TC] for f in range(8)]
        t_st_d = ln_stats(qft, [t_dq_toks[-1]])
        t_stats_d_prev = t_st_d
        t_bce_d = ln_tiny_bcast(t_st_d)

        def d_zf(e):
            last = None
            for f in range(8):
                e.tensor_sub(out=TA32[:, :], in0=QF[:, f * TC:(f + 1) * TC],
                             in1=MB[:, :])
                last = e.tensor_mul(out=QF[:, f * TC:(f + 1) * TC],
                                    in0=TA32[:, :], in1=IB[:, :])
            return last
        t_zf = st("dve", d_zf, deps=[t_bce_d, t_st_d])

        for f in range(8):
            slot = (f % 2) * TC

            def d_ow(e, f=f, c=c, slot=slot):
                return e.scalar_tensor_tensor(out=XF[:, slot:slot + TC],
                                              in0=QF[:, f * TC:(f + 1) * TC],
                                              scalar=0.0, in1=x2t(f, c),
                                              op0=ALU.bypass, op1=ALU.add)
            t_ow = st("dve", d_ow, deps=[t_zf, t_os_ping[f % 2]])

            def d_os(e, f=f, c=c, slot=slot):
                return [e.dma_start(out=outt[f * 128:(f + 1) * 128, c * TC:(c + 1) * TC],
                                    in_=XF[:, slot:slot + TC])]
            t_os = st("gp", d_os, deps=[t_ow], ndma=1)
            t_os_ping[f % 2] = t_os

    sch._wait("sy", "qgp", sch.cnt["qgp"])
    sch._wait("sy", "qsy", sch.cnt["qsy"])
    return _finish(nc, sch, H, es)


def _finish(nc, sch, H, es):
    for sem in ("spe", "sact", "sdve", "sgpc", "scc"):
        sch._wait("sy", sem, sch.cnt[sem])
    with nc.Block() as block:
        @block.tensor
        def _(e):
            sch.replay("pe", e, H)

        @block.scalar
        def _(e):
            sch.replay("act", e, H)

        @block.vector
        def _(e):
            sch.replay("dve", e, H)

        @block.sync
        def _(e):
            sch.replay("sy", e, H)

        @block.gpsimd
        def _(e):
            sch.replay("gp", e, H)

    es.close()
    return nc


def make_in_maps(inputs, TCORE):
    x = np.ascontiguousarray(np.asarray(inputs["L0_attn_x"], dtype=np.float32))
    B, T, C_ = x.shape
    ncore = 8
    nper = ncore // B
    assert T % nper == 0 and TCORE == T // nper
    shared = {
        "w1q": np.ascontiguousarray(inputs["L0_attn_W1q"], dtype=np.float32),
        "w2q": np.ascontiguousarray(inputs["L0_attn_W2q"], dtype=np.float32),
        "w1k": np.ascontiguousarray(inputs["L0_attn_W1k"], dtype=np.float32),
        "w2k": np.ascontiguousarray(inputs["L0_attn_W2k"], dtype=np.float32),
        "w1v": np.ascontiguousarray(inputs["L0_attn_W1v"], dtype=np.float32),
        "w2v": np.ascontiguousarray(inputs["L0_attn_W2v"], dtype=np.float32),
        "bq1": np.ascontiguousarray(inputs["L0_attn_b1q"], dtype=np.float32),
        "bq2": np.ascontiguousarray(inputs["L0_attn_b2q"], dtype=np.float32),
        "bk1": np.ascontiguousarray(inputs["L0_attn_b1k"], dtype=np.float32),
        "bk2": np.ascontiguousarray(inputs["L0_attn_b2k"], dtype=np.float32),
        "bv1": np.ascontiguousarray(inputs["L0_attn_b1v"], dtype=np.float32),
        "bv2": np.ascontiguousarray(inputs["L0_attn_b2v"], dtype=np.float32),
        "lnw": np.ascontiguousarray(inputs["L0_attn_ln_w"], dtype=np.float32),
        "lnb": np.ascontiguousarray(inputs["L0_attn_ln_b"], dtype=np.float32),
        "wint": np.ascontiguousarray(inputs["L0_ffn_weight_int8"], dtype=np.int8),
        "wsc": np.ascontiguousarray(
            np.asarray(inputs["L0_ffn_weight_scale"], dtype=np.float32).reshape(-1)),
        "fbi": np.ascontiguousarray(inputs["L0_ffn_bias"], dtype=np.float32),
        "idm": np.eye(128, dtype=np.float32),
        "trm": np.triu(np.ones((128, 128), dtype=np.float32)),
    }
    in_maps = []
    for i in range(ncore):
        b, h = i // nper, i % nper
        m = dict(shared)
        m["xt"] = np.ascontiguousarray(x[b, h * TCORE:(h + 1) * TCORE, :].T)
        flag = 1.0 if h > 0 else 0.0
        m["fme"] = np.full((128, 8), 1.0 - flag, dtype=np.float32)
        m["fmo"] = np.full((128, 8), flag, dtype=np.float32)
        in_maps.append(m)
    return in_maps, (B, T, C_, nper)


def kernel(**inputs):
    x = np.asarray(inputs["L0_attn_x"])
    B, T, C_ = x.shape
    nper = 8 // B
    TCORE = T // nper
    TC = 256
    nc = build_program(TCORE, TC)
    in_maps, (B, T, C_, nper) = make_in_maps(inputs, TCORE)
    res = run_bass_kernel_spmd(nc, in_maps, core_ids=list(range(8)))
    out = np.empty((B, T, C_), dtype=np.float32)
    for i in range(8):
        b, h = i // nper, i % nper
        out[b, h * TCORE:(h + 1) * TCORE, :] = \
            np.asarray(res.results[i]["outt"], dtype=np.float32).T
    return out
